# version 03 (final)
# baseline (speedup 1.0000x reference)
import math
import numpy as np
import ml_dtypes
from contextlib import ExitStack

import concourse.bass as bass
import concourse.mybir as mybir
from concourse.bass_utils import run_bass_kernel_spmd

F32 = mybir.dt.float32
BF16 = mybir.dt.bfloat16
AX = mybir.AxisListType.X
OP = mybir.AluOpType
AF = mybir.ActivationFunctionType
PI = math.pi
MAG = 12582912.0
BFNP = ml_dtypes.bfloat16

NC = 8
B, L, D = 16, 2048, 1024
TPC = B * L // NC
NF = 2176


class S:
    def __init__(self, h):
        self.h = h
        self.n = 0


class Pr:

    def __init__(self, nc, es):
        self.nc = nc
        self.q = {"sync": [], "scalar": [], "vector": [], "tensor": [], "gpsimd": []}
        self.sems = {k: S(es.enter_context(nc.semaphore(f"sem_{k}")))
                     for k in ["sync", "scalar", "vector", "tensor", "gpsimd"]}

    def _op(self, eng, emit, amt):
        waits = tuple((s.h, s.n) for s in self.sems.values() if s.n > 0)
        sem = self.sems[eng]
        sem.n += amt
        self.q[eng].append((emit, waits, ((sem.h, amt),)))

    def ld(self, emit):
        self._op("sync", emit, 16)

    def st(self, emit):
        self._op("gpsimd", emit, 16)

    def s(self, emit):
        self._op("scalar", emit, 1)

    def v(self, emit):
        self._op("vector", emit, 1)

    def t(self, emit):
        self._op("tensor", emit, 1)

    def finish(self):
        stot = self.sems["gpsimd"]
        self.q["gpsimd"].append(
            (lambda e: e.engine_nop(), ((stot.h, stot.n),), ()))
        with self.nc.Block() as blk:
            for eng, ops in self.q.items():
                if not ops:
                    continue

                def make(ops=ops):
                    def body(e):
                        for emit, waits, incs in ops:
                            for h, v in waits:
                                e.wait_ge(h, v)
                            ins = emit(e)
                            for h, a in incs:
                                r = ins.then_inc(h, a)
                                ins = r if r is not None else ins

                    return body

                getattr(blk, eng)(make())


def _ln_tile(p, src, dst, scr, col, epsc):
    p.v(lambda e: e.reduce_sum(col[:, 0:1], src[:, :], AX))
    p.v(lambda e: e.tensor_scalar_mul(col[:, 1:2], col[:, 0:1], 1.0 / D))
    p.v(lambda e: e.tensor_scalar(dst[:, :], src[:, :], col[:, 1:2], None, OP.subtract))
    p.s(lambda e: e.activation(scr[:, :], dst[:, :], AF.Square))
    p.v(lambda e: e.reduce_sum(col[:, 2:3], scr[:, :], AX))
    p.s(lambda e: e.activation(col[:, 3:4], col[:, 2:3], AF.Sqrt, bias=epsc, scale=1.0 / D))
    p.v(lambda e: e.reciprocal(col[:, 4:5], col[:, 3:4]))
    p.v(lambda e: e.tensor_scalar_mul(dst[:, :], dst[:, :], col[:, 4:5]))


def build_ln0():
    nc = bass.Bass("TRN2", target_bir_lowering=False)
    x = nc.dram_tensor("x", [TPC, D], F32, kind="ExternalInput").ap()
    y = nc.dram_tensor("y", [TPC, D], F32, kind="ExternalOutput").ap()
    es = ExitStack()
    xt = es.enter_context(nc.sbuf_tensor("xt", [128, D], F32))
    yt = es.enter_context(nc.sbuf_tensor("yt", [128, D], F32))
    sq = es.enter_context(nc.sbuf_tensor("sq", [128, D], F32))
    col = es.enter_context(nc.sbuf_tensor("col", [128, 8], F32))
    p = Pr(nc, es)
    p.v(lambda e: e.memset(col[:, 7:8], 1e-5))
    for i in range(TPC // 128):
        p.ld(lambda e, i=i: e.dma_start(out=xt[:, :], in_=x[i * 128:(i + 1) * 128, :]))
        _ln_tile(p, xt, yt, sq, col, col[:, 7:8])
        p.st(lambda e, i=i: e.dma_start(out=y[i * 128:(i + 1) * 128, :], in_=yt[:, :]))
    p.finish()
    es.close()
    return nc


def build_ssm():
    nc = bass.Bass("TRN2", target_bir_lowering=False)
    xh = nc.dram_tensor("xh", [L, 2048], F32, kind="ExternalInput").ap()
    dfc = nc.dram_tensor("dfc", [L, NF], F32, kind="ExternalInput").ap()
    dfs = nc.dram_tensor("dfs", [L, NF], F32, kind="ExternalInput").ap()
    icr = nc.dram_tensor("icr", [NF, L], F32, kind="ExternalInput").ap()
    ici = nc.dram_tensor("ici", [NF, L], F32, kind="ExternalInput").ap()
    logdt = nc.dram_tensor("logdt", [128, 1], F32, kind="ExternalInput").ap()
    logA = nc.dram_tensor("logA", [128, 32], F32, kind="ExternalInput").ap()
    AIm = nc.dram_tensor("AIm", [128, 32], F32, kind="ExternalInput").ap()
    CRe = nc.dram_tensor("CRe", [128, 32], F32, kind="ExternalInput").ap()
    CIm = nc.dram_tensor("CIm", [128, 32], F32, kind="ExternalInput").ap()
    Dc = nc.dram_tensor("Dc", [128, 1], F32, kind="ExternalInput").ap()
    iot = nc.dram_tensor("iot", [128, L], F32, kind="ExternalInput").ap()
    eye = nc.dram_tensor("eye", [128, 128], F32, kind="ExternalInput").ap()
    st = nc.dram_tensor("st", [L, 2048], F32, kind="ExternalOutput").ap()

    es = ExitStack()
    sbt = lambda n, sh, dt=F32: es.enter_context(nc.sbuf_tensor(n, sh, dt))
    pst = lambda n, sh, dt=F32: es.enter_context(nc.psum_tensor(n, sh, dt))
    prm = sbt("prm", [128, 704])
    K = sbt("K", [128, L])
    mag = sbt("mag", [128, L])
    snl = sbt("snl", [128, L])
    csl = sbt("csl", [128, L])
    phh = sbt("phh", [128, L])
    tsc = sbt("tsc", [128, L])
    KT = sbt("KT", [128, 16 * 128])
    KfR = sbt("KfR", [128, 17 * 128])
    KfI = sbt("KfI", [128, 17 * 128])
    dfct = [sbt(f"dfct{i}", [128, 128]) for i in range(16)]
    dfst = [sbt(f"dfst{i}", [128, 128]) for i in range(16)]
    icrt = sbt("icrt", [128, L])
    icit = sbt("icit", [128, L])
    xs = [sbt(f"xs{i}", [128, 512]) for i in range(16)]
    Y = [sbt(f"Y{i}", [128, 512]) for i in range(16)]
    eyes = sbt("eyes", [128, 128])
    iots = sbt("iots", [128, L])
    pw = [sbt(f"pw{i}", [128, 512]) for i in range(4)]
    pT = pst("pT", [128, 128])
    pKr = pst("pKr", [128, 128])
    pKi = pst("pKi", [128, 128])
    pXr = pst("pXr", [128, 512])
    pXi = pst("pXi", [128, 512])
    pY = pst("pY", [128, 512])
    p = Pr(nc, es)

    p.ld(lambda e: e.dma_start(out=prm[:, 0:1], in_=logdt[:, :]))
    p.ld(lambda e: e.dma_start(out=prm[:, 32:64], in_=logA[:, :]))
    p.ld(lambda e: e.dma_start(out=prm[:, 64:96], in_=AIm[:, :]))
    p.ld(lambda e: e.dma_start(out=prm[:, 96:128], in_=CRe[:, :]))
    p.ld(lambda e: e.dma_start(out=prm[:, 128:160], in_=CIm[:, :]))
    p.ld(lambda e: e.dma_start(out=prm[:, 1:2], in_=Dc[:, :]))
    p.ld(lambda e: e.dma_start(out=iots[:, :], in_=iot[:, :]))
    p.ld(lambda e: e.dma_start(out=eyes[:, :], in_=eye[:, :]))
    dt_c = prm[:, 2:3]
    npic = prm[:, 3:4]
    p.v(lambda e: e.memset(npic, -PI))
    p.s(lambda e: e.activation(dt_c, prm[:, 0:1], AF.Exp))
    p.s(lambda e: e.activation(prm[:, 544:576], prm[:, 32:64], AF.Exp))
    p.v(lambda e: e.tensor_scalar_mul(prm[:, 544:576], prm[:, 544:576], -1.0))
    p.v(lambda e: e.tensor_scalar(prm[:, 160:192], prm[:, 544:576], dt_c, None, OP.mult))
    p.v(lambda e: e.tensor_scalar(prm[:, 192:224], prm[:, 64:96], dt_c, None, OP.mult))
    def rred(dst, xsrc, scratch, cos):
        p.v(lambda e: e.tensor_scalar(dst, xsrc, 1.0 / (2 * PI), (0.25 if cos else 0.0), OP.mult, OP.add))
        p.v(lambda e: e.tensor_scalar(dst, dst, MAG, -MAG, OP.add, OP.add))
        p.v(lambda e: e.tensor_scalar(scratch, dst, 2 * PI, None, OP.mult))
        if cos:
            p.v(lambda e: e.tensor_scalar(dst, xsrc, PI / 2, None, OP.add))
            p.v(lambda e: e.tensor_sub(dst, dst, scratch))
        else:
            p.v(lambda e: e.tensor_sub(dst, xsrc, scratch))
        p.v(lambda e: e.tensor_scalar_max(dst, dst, -PI))
        p.v(lambda e: e.tensor_scalar_min(dst, dst, PI))

    rred(prm[:, 256:288], prm[:, 192:224], prm[:, 672:704], False)
    rred(prm[:, 288:320], prm[:, 192:224], prm[:, 672:704], True)
    p.s(lambda e: e.activation(prm[:, 224:256], prm[:, 160:192], AF.Exp))
    p.s(lambda e: e.activation(prm[:, 320:352], prm[:, 256:288], AF.Sin))
    p.s(lambda e: e.activation(prm[:, 352:384], prm[:, 288:320], AF.Sin))
    p.v(lambda e: e.tensor_mul(prm[:, 384:416], prm[:, 224:256], prm[:, 352:384]))
    p.v(lambda e: e.tensor_scalar_add(prm[:, 384:416], prm[:, 384:416], -1.0))
    p.v(lambda e: e.tensor_mul(prm[:, 416:448], prm[:, 224:256], prm[:, 320:352]))
    p.v(lambda e: e.tensor_mul(prm[:, 448:480], prm[:, 544:576], prm[:, 544:576]))
    p.v(lambda e: e.tensor_mul(prm[:, 576:608], prm[:, 64:96], prm[:, 64:96]))
    p.v(lambda e: e.tensor_add(prm[:, 448:480], prm[:, 448:480], prm[:, 576:608]))
    p.v(lambda e: e.reciprocal(prm[:, 448:480], prm[:, 448:480]))
    p.v(lambda e: e.tensor_mul(prm[:, 480:512], prm[:, 384:416], prm[:, 544:576]))
    p.v(lambda e: e.tensor_mul(prm[:, 576:608], prm[:, 416:448], prm[:, 64:96]))
    p.v(lambda e: e.tensor_add(prm[:, 480:512], prm[:, 480:512], prm[:, 576:608]))
    p.v(lambda e: e.tensor_mul(prm[:, 480:512], prm[:, 480:512], prm[:, 448:480]))
    p.v(lambda e: e.tensor_mul(prm[:, 512:544], prm[:, 416:448], prm[:, 544:576]))
    p.v(lambda e: e.tensor_mul(prm[:, 576:608], prm[:, 384:416], prm[:, 64:96]))
    p.v(lambda e: e.tensor_sub(prm[:, 512:544], prm[:, 512:544], prm[:, 576:608]))
    p.v(lambda e: e.tensor_mul(prm[:, 512:544], prm[:, 512:544], prm[:, 448:480]))
    p.v(lambda e: e.tensor_mul(prm[:, 608:640], prm[:, 96:128], prm[:, 480:512]))
    p.v(lambda e: e.tensor_mul(prm[:, 576:608], prm[:, 128:160], prm[:, 512:544]))
    p.v(lambda e: e.tensor_sub(prm[:, 608:640], prm[:, 608:640], prm[:, 576:608]))
    p.v(lambda e: e.tensor_mul(prm[:, 640:672], prm[:, 96:128], prm[:, 512:544]))
    p.v(lambda e: e.tensor_mul(prm[:, 576:608], prm[:, 128:160], prm[:, 480:512]))
    p.v(lambda e: e.tensor_add(prm[:, 640:672], prm[:, 640:672], prm[:, 576:608]))
    p.v(lambda e: e.memset(K[:, :], 0.0))
    CwR, CwI = prm[:, 608:640], prm[:, 640:672]
    dAR, dAI = prm[:, 160:192], prm[:, 192:224]
    for n in range(32):
        p.v(lambda e, n=n: e.tensor_scalar(phh[:, :], iots[:, :], dAI[:, n:n + 1], None, OP.mult))
        rred(snl[:, :], phh[:, :], tsc[:, :], False)
        rred(csl[:, :], phh[:, :], tsc[:, :], True)
        p.s(lambda e, n=n: e.activation(mag[:, :], iots[:, :], AF.Exp, scale=dAR[:, n:n + 1]))
        p.s(lambda e: e.activation(snl[:, :], snl[:, :], AF.Sin))
        p.s(lambda e: e.activation(csl[:, :], csl[:, :], AF.Sin))
        p.v(lambda e: e.tensor_mul(csl[:, :], mag[:, :], csl[:, :]))
        p.v(lambda e: e.tensor_mul(snl[:, :], mag[:, :], snl[:, :]))
        p.v(lambda e, n=n: e.tensor_scalar(tsc[:, :], csl[:, :], CwR[:, n:n + 1], 2.0, OP.mult, OP.mult))
        p.v(lambda e: e.tensor_add(K[:, :], K[:, :], tsc[:, :]))
        p.v(lambda e, n=n: e.tensor_scalar(tsc[:, :], snl[:, :], CwI[:, n:n + 1], 2.0, OP.mult, OP.mult))
        p.v(lambda e: e.tensor_sub(K[:, :], K[:, :], tsc[:, :]))
    p.v(lambda e: e.tensor_add(K[:, 0:1], K[:, 0:1], prm[:, 1:2]))
    for lt in range(16):
        p.t(lambda e, lt=lt: e.transpose(pT[:, :], K[:, lt * 128:(lt + 1) * 128], eyes[:, :]))
        p.v(lambda e, lt=lt: e.tensor_copy(KT[:, lt * 128:(lt + 1) * 128], pT[:, :]))
    for ft in range(17):
        for lt in range(16):
            p.ld(lambda e, ft=ft, lt=lt: e.dma_start(
                out=dfct[lt][:, :], in_=dfc[lt * 128:(lt + 1) * 128, ft * 128:(ft + 1) * 128]))
            p.ld(lambda e, ft=ft, lt=lt: e.dma_start(
                out=dfst[lt][:, :], in_=dfs[lt * 128:(lt + 1) * 128, ft * 128:(ft + 1) * 128]))
        for lt in range(16):
            p.t(lambda e, lt=lt: e.matmul(pKr[:, :], dfct[lt][:, :], KT[:, lt * 128:(lt + 1) * 128],
                                          start=(lt == 0), stop=(lt == 15)))
        for lt in range(16):
            p.t(lambda e, lt=lt: e.matmul(pKi[:, :], dfst[lt][:, :], KT[:, lt * 128:(lt + 1) * 128],
                                          start=(lt == 0), stop=(lt == 15)))
        p.v(lambda e, ft=ft: e.tensor_copy(KfR[:, ft * 128:(ft + 1) * 128], pKr[:, :]))
        p.v(lambda e, ft=ft: e.tensor_copy(KfI[:, ft * 128:(ft + 1) * 128], pKi[:, :]))

    def bc(tile, ft):
        a = tile[:, ft * 128:(ft + 1) * 128]
        return bass.AP(a.tensor, a.offset, [list(a.ap[0]), [0, 4], list(a.ap[1])])

    for rc in range(4):
        for lt in range(16):
            p.ld(lambda e, rc=rc, lt=lt: e.dma_start(
                out=xs[lt][:, :], in_=xh[lt * 128:(lt + 1) * 128, rc * 512:(rc + 1) * 512]))
            p.v(lambda e, lt=lt: e.memset(Y[lt][:, :], 0.0))
        for ft in range(17):
            for lt in range(16):
                p.ld(lambda e, ft=ft, lt=lt: e.dma_start(
                    out=dfct[lt][:, :], in_=dfc[lt * 128:(lt + 1) * 128, ft * 128:(ft + 1) * 128]))
                p.ld(lambda e, ft=ft, lt=lt: e.dma_start(
                    out=dfst[lt][:, :], in_=dfs[lt * 128:(lt + 1) * 128, ft * 128:(ft + 1) * 128]))
            p.ld(lambda e, ft=ft: e.dma_start(out=icrt[:, :], in_=icr[ft * 128:(ft + 1) * 128, :]))
            p.ld(lambda e, ft=ft: e.dma_start(out=icit[:, :], in_=ici[ft * 128:(ft + 1) * 128, :]))
            for lt in range(16):
                p.t(lambda e, lt=lt: e.matmul(pXr[:, :], dfct[lt][:, :], xs[lt][:, :],
                                              start=(lt == 0), stop=(lt == 15)))
            for lt in range(16):
                p.t(lambda e, lt=lt: e.matmul(pXi[:, :], dfst[lt][:, :], xs[lt][:, :],
                                              start=(lt == 0), stop=(lt == 15)))
            xr3 = pXr[:, :].rearrange("p (b h) -> p b h", b=4)
            xi3 = pXi[:, :].rearrange("p (b h) -> p b h", b=4)
            t13 = pw[2][:, :].rearrange("p (b h) -> p b h", b=4)
            t23 = pw[3][:, :].rearrange("p (b h) -> p b h", b=4)
            p.v(lambda e, ft=ft, t13=t13, xr3=xr3: e.tensor_tensor(t13, xr3, bc(KfR, ft), OP.mult))
            p.v(lambda e, ft=ft, t23=t23, xi3=xi3: e.tensor_tensor(t23, xi3, bc(KfI, ft), OP.mult))
            p.v(lambda e: e.tensor_sub(pw[0][:, :], pw[2][:, :], pw[3][:, :]))
            p.v(lambda e, ft=ft, t13=t13, xr3=xr3: e.tensor_tensor(t13, xr3, bc(KfI, ft), OP.mult))
            p.v(lambda e, ft=ft, t23=t23, xi3=xi3: e.tensor_tensor(t23, xi3, bc(KfR, ft), OP.mult))
            p.v(lambda e: e.tensor_add(pw[1][:, :], pw[2][:, :], pw[3][:, :]))
            for lt in range(16):
                p.t(lambda e, lt=lt: e.matmul(pY[:, :], icrt[:, lt * 128:(lt + 1) * 128],
                                              pw[0][:, :], start=True, stop=False))
                p.t(lambda e, lt=lt: e.matmul(pY[:, :], icit[:, lt * 128:(lt + 1) * 128],
                                              pw[1][:, :], start=False, stop=True))
                p.v(lambda e, lt=lt: e.tensor_add(Y[lt][:, :], Y[lt][:, :], pY[:, :]))
        for lt in range(16):
            p.st(lambda e, rc=rc, lt=lt: e.dma_start(
                out=st[lt * 128:(lt + 1) * 128, rc * 512:(rc + 1) * 512], in_=Y[lt][:, :]))
    p.finish()
    es.close()
    return nc


def build_moe():
    nc = bass.Bass("TRN2", target_bir_lowering=False)
    xnT = nc.dram_tensor("xnT", [D, TPC], F32, kind="ExternalInput").ap()
    x0 = nc.dram_tensor("x0", [TPC, D], F32, kind="ExternalInput").ap()
    s0 = nc.dram_tensor("s0", [TPC, D], F32, kind="ExternalInput").ap()
    W1 = nc.dram_tensor("W1", [9 * 1024, 1408], F32, kind="ExternalInput").ap()
    W3 = nc.dram_tensor("W3", [9 * 1024, 1408], F32, kind="ExternalInput").ap()
    W2 = nc.dram_tensor("W2", [9 * 1408, 1024], F32, kind="ExternalInput").ap()
    B1 = nc.dram_tensor("B1", [9 * 128, 11], F32, kind="ExternalInput").ap()
    B3 = nc.dram_tensor("B3", [9 * 128, 11], F32, kind="ExternalInput").ap()
    B2 = nc.dram_tensor("B2", [9, 1024], F32, kind="ExternalInput").ap()
    GW = nc.dram_tensor("GW", [D, 8], F32, kind="ExternalInput").ap()
    GB = nc.dram_tensor("GB", [1, 8], F32, kind="ExternalInput").ap()
    ONE = nc.dram_tensor("ONE", [128, 128], F32, kind="ExternalInput").ap()
    x1o = nc.dram_tensor("x1o", [TPC, D], F32, kind="ExternalOutput").ap()
    xn1o = nc.dram_tensor("xn1o", [TPC, D], F32, kind="ExternalOutput").ap()

    es = ExitStack()
    sbt = lambda n, sh, dt=F32: es.enter_context(nc.sbuf_tensor(n, sh, dt))
    pst = lambda n, sh, dt=F32: es.enter_context(nc.psum_tensor(n, sh, dt))
    TB = 1024
    xbf = [sbt(f"xbf{i}", [128, TB], BF16) for i in range(8)]
    w1b = [sbt(f"w1b{i}", [128, 1408], BF16) for i in range(8)]
    w3b = [sbt(f"w3b{i}", [128, 1408], BF16) for i in range(8)]
    w2b = [sbt(f"w2b{i}", [128, 1024], BF16) for i in range(11)]
    stg = sbt("stg", [128, 1408], F32)
    hbf = [sbt(f"hbf{i}", [128, 512], BF16) for i in range(11)]
    hs0 = sbt("hs0", [128, 512], F32)
    hs1 = sbt("hs1", [128, 512], F32)
    hs2 = sbt("hs2", [128, 512], F32)
    acc = [sbt(f"acc{i}", [128, 1024], F32) for i in range(8)]
    bcols = sbt("bcols", [128, 22])
    b2bc = sbt("b2bc", [128, 1024], F32)
    b2row = sbt("b2row", [1, 1024], F32)
    gws = sbt("gws", [128, 64], F32)
    gbbc = sbt("gbbc", [128, 8], F32)
    ones = sbt("ones", [128, 128], F32)
    rws = [sbt(f"rws{i}", [128, 8], F32) for i in range(8)]
    rsc = sbt("rsc", [128, 24], F32)
    xr = sbt("xr", [128, 128], F32)
    cmb0 = sbt("cmb0", [128, 1024], F32)
    cmb1 = sbt("cmb1", [128, 1024], F32)
    cmb2 = sbt("cmb2", [128, 1024], F32)
    col = sbt("col", [128, 8], F32)
    epsc = sbt("epsc", [128, 1], F32)
    pA = pst("pA", [128, 512])
    pB = pst("pB", [128, 512])
    pC = pst("pC", [128, 512])
    pR = pst("pR", [128, 8])
    pBC = pst("pBC", [128, 512])
    p = Pr(nc, es)

    p.v(lambda e: e.memset(epsc[:, :], 1e-5))
    p.ld(lambda e: e.dma_start(out=ones[:, :], in_=ONE[:, :]))
    for dt_i in range(8):
        p.ld(lambda e, dt_i=dt_i: e.dma_start(out=gws[:, dt_i * 8:(dt_i + 1) * 8],
                                              in_=GW[dt_i * 128:(dt_i + 1) * 128, :]))
    p.ld(lambda e: e.dma_start(out=b2row[0:1, 0:8], in_=GB[:, :]))
    p.t(lambda e: e.matmul(pR[:, :], ones[0:1, :], b2row[0:1, 0:8], start=True, stop=True))
    p.v(lambda e: e.tensor_copy(gbbc[:, :], pR[:, :]))

    for tb in range(TPC // TB):
        for ts_ in range(8):
            t0 = tb * TB + ts_ * 128
            for dt_i in range(8):
                p.ld(lambda e, dt_i=dt_i, t0=t0: e.dma_start(
                    out=xr[:, :], in_=xnT[dt_i * 128:(dt_i + 1) * 128, t0:t0 + 128]))
                p.t(lambda e, dt_i=dt_i: e.matmul(pR[:, :], xr[:, :], gws[:, dt_i * 8:(dt_i + 1) * 8],
                                                  start=(dt_i == 0), stop=(dt_i == 7)))
            p.v(lambda e: e.reduce_max(rsc[:, 0:1], pR[:, :], AX))
            p.v(lambda e: e.tensor_scalar_mul(rsc[:, 1:2], rsc[:, 0:1], -1.0))
            p.s(lambda e, ts_=ts_: e.activation(rws[ts_][:, :], pR[:, :], AF.Exp, bias=rsc[:, 1:2]))
            p.v(lambda e, ts_=ts_: e.reduce_sum(rsc[:, 2:3], rws[ts_][:, :], AX))
            p.v(lambda e: e.reciprocal(rsc[:, 3:4], rsc[:, 2:3]))
            p.v(lambda e, ts_=ts_: e.tensor_scalar_mul(rws[ts_][:, :], rws[ts_][:, :], rsc[:, 3:4]))
            p.v(lambda e, ts_=ts_: e.tensor_add(rsc[:, 8:16], rws[ts_][:, :], gbbc[:, :]))
            p.v(lambda e: e.reduce_max(rsc[:, 0:1], rsc[:, 8:16], AX))
            p.v(lambda e: e.tensor_scalar(rsc[:, 16:24], rsc[:, 8:16], rsc[:, 0:1], None, OP.is_ge))
            p.v(lambda e: e.tensor_scalar(col[:, 0:8], rsc[:, 16:24], -1e30, None, OP.mult))
            p.v(lambda e: e.tensor_add(rsc[:, 8:16], rsc[:, 8:16], col[:, 0:8]))
            p.v(lambda e: e.reduce_max(rsc[:, 0:1], rsc[:, 8:16], AX))
            p.v(lambda e: e.tensor_scalar(rsc[:, 8:16], rsc[:, 8:16], rsc[:, 0:1], None, OP.is_ge))
            p.v(lambda e: e.tensor_add(rsc[:, 16:24], rsc[:, 16:24], rsc[:, 8:16]))
            p.v(lambda e, ts_=ts_: e.tensor_mul(rws[ts_][:, :], rws[ts_][:, :], rsc[:, 16:24]))
        for dt_i in range(8):
            p.ld(lambda e, dt_i=dt_i, tb=tb: e.dma_start(
                out=stg[:, 0:TB], in_=xnT[dt_i * 128:(dt_i + 1) * 128, tb * TB:(tb + 1) * TB]))
            p.s(lambda e, dt_i=dt_i: e.activation(xbf[dt_i][:, :], stg[:, 0:TB], AF.Identity))
        for ts_ in range(8):
            p.v(lambda e, ts_=ts_: e.memset(acc[ts_][:, :], 0.0))
        for ex in range(9):
            p.ld(lambda e, ex=ex: e.dma_start(out=bcols[:, 0:11], in_=B1[ex * 128:(ex + 1) * 128, :]))
            p.ld(lambda e, ex=ex: e.dma_start(out=bcols[:, 11:22], in_=B3[ex * 128:(ex + 1) * 128, :]))
            p.ld(lambda e, ex=ex: e.dma_start(out=b2row[0:1, :], in_=B2[ex:ex + 1, :]))
            for dt_i in range(8):
                p.ld(lambda e, ex=ex, dt_i=dt_i: e.dma_start(
                    out=stg[:, :], in_=W1[ex * 1024 + dt_i * 128: ex * 1024 + (dt_i + 1) * 128, :]))
                p.s(lambda e, dt_i=dt_i: e.activation(w1b[dt_i][:, :], stg[:, :], AF.Identity))
            for dt_i in range(8):
                p.ld(lambda e, ex=ex, dt_i=dt_i: e.dma_start(
                    out=stg[:, :], in_=W3[ex * 1024 + dt_i * 128: ex * 1024 + (dt_i + 1) * 128, :]))
                p.s(lambda e, dt_i=dt_i: e.activation(w3b[dt_i][:, :], stg[:, :], AF.Identity))
            for ht in range(11):
                p.ld(lambda e, ex=ex, ht=ht: e.dma_start(
                    out=stg[:, 0:1024], in_=W2[ex * 1408 + ht * 128: ex * 1408 + (ht + 1) * 128, :]))
                p.s(lambda e, ht=ht: e.activation(w2b[ht][:, :], stg[:, 0:1024], AF.Identity))
            p.t(lambda e: e.matmul(pBC[:, :], ones[0:1, :], b2row[0:1, 0:512], start=True, stop=True))
            p.v(lambda e: e.tensor_copy(b2bc[:, 0:512], pBC[:, :]))
            p.t(lambda e: e.matmul(pBC[:, :], ones[0:1, :], b2row[0:1, 512:1024], start=True, stop=True))
            p.v(lambda e: e.tensor_copy(b2bc[:, 512:1024], pBC[:, :]))
            for tc in range(TB // 512):
                for ht in range(11):
                    for dt_i in range(8):
                        p.t(lambda e, ht=ht, dt_i=dt_i, tc=tc: e.matmul(
                            pA[:, :], w1b[dt_i][:, ht * 128:(ht + 1) * 128],
                            xbf[dt_i][:, tc * 512:(tc + 1) * 512], start=(dt_i == 0), stop=(dt_i == 7)))
                    for dt_i in range(8):
                        p.t(lambda e, ht=ht, dt_i=dt_i, tc=tc: e.matmul(
                            pB[:, :], w3b[dt_i][:, ht * 128:(ht + 1) * 128],
                            xbf[dt_i][:, tc * 512:(tc + 1) * 512], start=(dt_i == 0), stop=(dt_i == 7)))
                    p.s(lambda e, ht=ht: e.activation(hs0[:, :], pA[:, :], AF.Identity, bias=bcols[:, ht:ht + 1]))
                    p.v(lambda e, ht=ht: e.tensor_scalar(hs1[:, :], pB[:, :], bcols[:, 11 + ht:12 + ht],
                                                         None, OP.add))
                    p.v(lambda e: e.tensor_mul(hs2[:, :], hs0[:, :], hs1[:, :]))
                    p.s(lambda e, ht=ht: e.activation(hbf[ht][:, :], hs2[:, :], AF.Silu))
                for ts2 in range(4):
                    ts_ = tc * 4 + ts2
                    for dh in range(2):
                        for ht in range(11):
                            p.t(lambda e, ts2=ts2, dh=dh, ht=ht: e.matmul(
                                pC[:, :], hbf[ht][:, ts2 * 128:(ts2 + 1) * 128],
                                w2b[ht][:, dh * 512:(dh + 1) * 512], start=(ht == 0), stop=(ht == 10)))
                        p.v(lambda e, dh=dh: e.tensor_tensor(hs0[:, :], pC[:, :],
                                                             b2bc[:, dh * 512:(dh + 1) * 512], OP.add))
                        if ex < 8:
                            p.v(lambda e, ts_=ts_, ex=ex: e.tensor_scalar_mul(
                                hs0[:, :], hs0[:, :], rws[ts_][:, ex:ex + 1]))
                        p.v(lambda e, ts_=ts_, dh=dh: e.tensor_add(
                            acc[ts_][:, dh * 512:(dh + 1) * 512], acc[ts_][:, dh * 512:(dh + 1) * 512],
                            hs0[:, :]))
        for ts_ in range(8):
            t0 = tb * TB + ts_ * 128
            p.ld(lambda e, t0=t0: e.dma_start(out=cmb0[:, :], in_=x0[t0:t0 + 128, :]))
            p.ld(lambda e, t0=t0: e.dma_start(out=cmb1[:, :], in_=s0[t0:t0 + 128, :]))
            p.s(lambda e, ts_=ts_: e.activation(cmb2[:, :], acc[ts_][:, :], AF.Silu))
            p.v(lambda e: e.tensor_mul(cmb2[:, :], cmb1[:, :], cmb2[:, :]))
            p.v(lambda e: e.tensor_add(cmb2[:, :], cmb0[:, :], cmb2[:, :]))
            p.st(lambda e, t0=t0: e.dma_start(out=x1o[t0:t0 + 128, :], in_=cmb2[:, :]))
            _ln_tile(p, cmb2, cmb0, cmb1, col, epsc[:, :])
            p.st(lambda e, t0=t0: e.dma_start(out=xn1o[t0:t0 + 128, :], in_=cmb0[:, :]))
    p.finish()
    es.close()
    return nc


def build_attn():
    nc = bass.Bass("TRN2", target_bir_lowering=False)
    xnT = nc.dram_tensor("xnT", [D, TPC], F32, kind="ExternalInput").ap()
    Wq = nc.dram_tensor("Wq", [D, 1024], F32, kind="ExternalInput").ap()
    Wkv = nc.dram_tensor("Wkv", [D, 1024], F32, kind="ExternalInput").ap()
    Wo = nc.dram_tensor("Wo", [512, 1024], F32, kind="ExternalInput").ap()
    NW = nc.dram_tensor("NW", [128, 8], F32, kind="ExternalInput").ap()
    QG = nc.dram_tensor("QG", [128, 8], F32, kind="ExternalInput").ap()
    KG = nc.dram_tensor("KG", [128, 4], F32, kind="ExternalInput").ap()
    ONE = nc.dram_tensor("ONE", [128, 128], F32, kind="ExternalInput").ap()
    EYEF = nc.dram_tensor("EYEF", [128, 128], F32, kind="ExternalInput").ap()
    EYE2 = nc.dram_tensor("EYE2", [128, 64], BF16, kind="ExternalInput").ap()
    EYEB = nc.dram_tensor("EYEB", [128, 128], BF16, kind="ExternalInput").ap()
    MB = nc.dram_tensor("MB", [128, 4096], F32, kind="ExternalInput").ap()
    s1o = nc.dram_tensor("s1o", [TPC, D], F32, kind="ExternalOutput").ap()

    es = ExitStack()
    sbt = lambda n, sh, dt=F32: es.enter_context(nc.sbuf_tensor(n, sh, dt))
    pst = lambda n, sh, dt=F32: es.enter_context(nc.psum_tensor(n, sh, dt))
    wqb = [sbt(f"wqb{i}", [128, 1024], BF16) for i in range(8)]
    wkb = [sbt(f"wkb{i}", [128, 1024], BF16) for i in range(8)]
    wob = [sbt(f"wob{i}", [64, 1024], BF16) for i in range(8)]
    stg = sbt("stg", [128, 1024], F32)
    stg2 = sbt("stg2", [128, 512], F32)
    xb = [sbt(f"xb{i}", [128, 512], BF16) for i in range(8)]
    qn = [sbt(f"qn{i}", [128, 2048], BF16) for i in range(8)]
    kn = [sbt(f"kn{i}", [128, 2048], BF16) for i in range(4)]
    vT = [sbt(f"vT{i}", [128, 2048], BF16) for i in range(4)]
    vj = [sbt(f"vj{i}", [128, 1024], BF16) for i in range(8)]
    sim = sbt("sim", [128, 2048], F32)
    pbf = sbt("pbf", [128, 2048], BF16)
    ptall = sbt("ptall", [128, 16 * 128], BF16)
    atT = [sbt(f"atT{i}", [64, 128], BF16) for i in range(8)]
    mask = sbt("mask", [128, 4096], F32)
    onesb = sbt("onesb", [128, 128], F32)
    eyef = sbt("eyef", [128, 128], F32)
    eye2 = sbt("eye2", [128, 64], BF16)
    eyeb = sbt("eyeb", [128, 128], BF16)
    cols = sbt("cols", [128, 32], F32)
    nsc = sbt("nsc", [128, 8], F32)
    rb = sbt("rb", [128, 512], F32)
    rb2 = sbt("rb2", [128, 512], F32)
    sq = sbt("sq", [128, 512], F32)
    ot = sbt("ot", [128, 64], F32)
    pQ = pst("pQ", [128, 512])
    pS = [pst(f"pS{i}", [128, 512]) for i in range(2)]
    pPr = pst("pPr", [128, 512])
    pTb = pst("pTb", [128, 128], BF16)
    pO = pst("pO", [128, 64])
    pTr = pst("pTr", [128, 128])
    p = Pr(nc, es)

    LOQ = [0, 1, 4, 5, 8, 9, 12, 13]
    HIQ = [2, 3, 6, 7, 10, 11, 14, 15]
    tile_of = {}
    for i, h in enumerate(LOQ):
        tile_of[h] = (i, 0)
    for i, h in enumerate(HIQ):
        tile_of[h] = (i, 64)

    p.ld(lambda e: e.dma_start(out=onesb[:, :], in_=ONE[:, :]))
    p.ld(lambda e: e.dma_start(out=eyef[:, :], in_=EYEF[:, :]))
    p.ld(lambda e: e.dma_start(out=eye2[:, :], in_=EYE2[:, :]))
    p.ld(lambda e: e.dma_start(out=eyeb[:, :], in_=EYEB[:, :]))
    p.ld(lambda e: e.dma_start(out=mask[:, :], in_=MB[:, :]))
    p.ld(lambda e: e.dma_start(out=cols[:, 0:8], in_=NW[:, :]))
    p.ld(lambda e: e.dma_start(out=cols[:, 8:16], in_=QG[:, :]))
    p.ld(lambda e: e.dma_start(out=cols[:, 16:20], in_=KG[:, :]))
    p.v(lambda e: e.memset(nsc[:, 4:5], 8.0))
    p.v(lambda e: e.memset(nsc[:, 5:6], 1.1920929e-07))
    p.s(lambda e: e.activation(cols[:, 8:16], cols[:, 8:16], AF.Identity, bias=nsc[:, 4:5], scale=8.0))
    p.s(lambda e: e.activation(cols[:, 16:20], cols[:, 16:20], AF.Identity, bias=nsc[:, 4:5], scale=8.0))
    for dt_i in range(8):
        p.ld(lambda e, dt_i=dt_i: e.dma_start(out=stg[:, :], in_=Wq[dt_i * 128:(dt_i + 1) * 128, :]))
        p.v(lambda e, dt_i=dt_i: e.tensor_scalar_mul(stg[:, :], stg[:, :], cols[:, dt_i:dt_i + 1]))
        p.s(lambda e, dt_i=dt_i: e.activation(wqb[dt_i][:, :], stg[:, :], AF.Identity))
    for dt_i in range(8):
        p.ld(lambda e, dt_i=dt_i: e.dma_start(out=stg[:, :], in_=Wkv[dt_i * 128:(dt_i + 1) * 128, :]))
        p.v(lambda e, dt_i=dt_i: e.tensor_scalar_mul(stg[:, :], stg[:, :], cols[:, dt_i:dt_i + 1]))
        p.s(lambda e, dt_i=dt_i: e.activation(wkb[dt_i][:, :], stg[:, :], AF.Identity))
    for kvh in range(8):
        p.ld(lambda e, kvh=kvh: e.dma_start(out=stg[0:64, :], in_=Wo[kvh * 64:(kvh + 1) * 64, :]))
        p.s(lambda e, kvh=kvh: e.activation(wob[kvh][:, :], stg[0:64, :], AF.Identity))

    def headgroup(wsrc, colslot, ob, dstf, scale_col):
        for dt_i in range(8):
            p.t(lambda e, dt_i=dt_i, ob=ob, colslot=colslot, wsrc=wsrc: e.matmul(
                pQ[ob:ob + 64, :], wsrc[dt_i][:, colslot * 64:(colslot + 1) * 64], xb[dt_i][:, :],
                start=(dt_i == 0), stop=(dt_i == 7)))
        p.s(lambda e, ob=ob: e.activation(sq[ob:ob + 64, :], pQ[ob:ob + 64, :], AF.Square))
        p.t(lambda e, ob=ob: e.matmul(pS[0][0:1, :], onesb[ob:ob + 64, 0:1], sq[ob:ob + 64, :],
                                      start=True, stop=True))
        p.s(lambda e: e.activation(rb2[0:1, :], pS[0][0:1, :], AF.Sqrt))
        p.v(lambda e: e.tensor_scalar_max(rb2[0:1, :], rb2[0:1, :], 1e-12))
        p.v(lambda e: e.reciprocal(rb2[0:1, :], rb2[0:1, :]))
        p.t(lambda e, ob=ob: e.matmul(pS[1][ob:ob + 64, :], onesb[0:1, 0:64], rb2[0:1, :],
                                      start=True, stop=True))
        p.v(lambda e, ob=ob: e.tensor_copy(rb2[ob:ob + 64, :], pS[1][ob:ob + 64, :]))
        p.v(lambda e, ob=ob: e.tensor_mul(sq[ob:ob + 64, :], pQ[ob:ob + 64, :], rb2[ob:ob + 64, :]))
        p.v(lambda e, ob=ob, dstf=dstf, scale_col=scale_col: e.tensor_scalar_mul(
            dstf, sq[ob:ob + 64, :], scale_col))

    for bb in range(2):
        base = bb * 2048
        for tc in range(4):
            t0 = base + tc * 512
            for dt_i in range(8):
                p.ld(lambda e, dt_i=dt_i, t0=t0: e.dma_start(
                    out=stg2[:, :], in_=xnT[dt_i * 128:(dt_i + 1) * 128, t0:t0 + 512]))
                p.s(lambda e, dt_i=dt_i: e.activation(xb[dt_i][:, :], stg2[:, :], AF.Identity))
                p.s(lambda e, dt_i=dt_i: e.activation(sq[:, :], stg2[:, :], AF.Square))
                p.t(lambda e, dt_i=dt_i: e.matmul(pPr[0:1, :], onesb[:, 0:1], sq[:, :],
                                                  start=(dt_i == 0), stop=(dt_i == 7)))
            p.s(lambda e: e.activation(rb[0:1, :], pPr[0:1, :], AF.Sqrt, bias=nsc[0:1, 5:6], scale=1.0 / D))
            p.v(lambda e: e.reciprocal(rb2[0:1, :], rb[0:1, :]))
            p.t(lambda e: e.matmul(pPr[:, :], onesb[0:1, :], rb2[0:1, :], start=True, stop=True))
            p.v(lambda e: e.tensor_copy(rb[:, :], pPr[:, :]))
            cs = slice(tc * 512, (tc + 1) * 512)
            for qh in range(16):
                ti, ob = tile_of[qh]
                headgroup(wqb, qh, ob, qn[ti][ob:ob + 64, cs], cols[ob:ob + 64, 8 + qh // 2:9 + qh // 2])
            for kvh in range(8):
                ti, ob = kvh // 2, (kvh % 2) * 64
                headgroup(wkb, kvh, ob, kn[ti][ob:ob + 64, cs], cols[ob:ob + 64, 16 + kvh // 2:17 + kvh // 2])
            for kvh in range(8):
                ti, ob = kvh // 2, (kvh % 2) * 64
                for dt_i in range(8):
                    p.t(lambda e, dt_i=dt_i, ob=ob, kvh=kvh: e.matmul(
                        pQ[ob:ob + 64, :], wkb[dt_i][:, 512 + kvh * 64:512 + (kvh + 1) * 64], xb[dt_i][:, :],
                        start=(dt_i == 0), stop=(dt_i == 7)))
                p.v(lambda e, ob=ob: e.tensor_mul(sq[ob:ob + 64, :], pQ[ob:ob + 64, :], rb[ob:ob + 64, :]))
                p.s(lambda e, ti=ti, ob=ob, cs=cs: e.activation(vT[ti][ob:ob + 64, cs],
                                                                sq[ob:ob + 64, :], AF.Identity))
        p.v(lambda e: e.memset(sim[:, :], 0.0))
        for kvh in range(8):
            ct = kvh // 2
            po = (kvh % 2) * 64
            for jt in range(16):
                p.t(lambda e, ct=ct, po=po, jt=jt: e.transpose(
                    pTb[:, 0:64], vT[ct][po:po + 64, jt * 128:(jt + 1) * 128], eye2[po:po + 64, :]))
                p.v(lambda e, kvh=kvh, jt=jt: e.tensor_copy(vj[kvh][:, jt * 64:(jt + 1) * 64], pTb[:, 0:64]))
        for it in range(16):
            njc = it // 4 + 1
            njt = it + 1
            for kvh in range(8):
                kct = kvh // 2
                kpo = (kvh % 2) * 64
                for g in range(2):
                    qh = kvh * 2 + g
                    qti, qpo = tile_of[qh]
                    for jc in range(njc):
                        pb = pS[jc % 2]
                        p.t(lambda e, qti=qti, qpo=qpo, kct=kct, kpo=kpo, it=it, jc=jc, pb=pb: e.matmul(
                            pb[:, :], qn[qti][qpo:qpo + 64, it * 128:(it + 1) * 128],
                            kn[kct][kpo:kpo + 64, jc * 512:(jc + 1) * 512], start=True, stop=True))
                        p.s(lambda e, jc=jc, pb=pb: e.activation(sim[:, jc * 512:(jc + 1) * 512],
                                                                 pb[:, :], AF.Tanh, scale=0.02))
                    p.v(lambda e, nj=njc * 512: e.tensor_scalar_mul(sim[:, 0:nj], sim[:, 0:nj], 6.25))
                    p.v(lambda e, it=it: e.tensor_add(sim[:, :], sim[:, :],
                                                      mask[:, 2048 - it * 128:4096 - it * 128]))
                    p.v(lambda e: e.reduce_max(nsc[:, 0:1], sim[:, :], AX))
                    p.v(lambda e: e.tensor_scalar_mul(nsc[:, 1:2], nsc[:, 0:1], -1.0))
                    p.s(lambda e: e.activation(sim[:, :], sim[:, :], AF.Exp, bias=nsc[:, 1:2]))
                    p.v(lambda e: e.reduce_sum(nsc[:, 2:3], sim[:, :], AX))
                    p.v(lambda e: e.reciprocal(nsc[:, 3:4], nsc[:, 2:3]))
                    p.v(lambda e: e.tensor_scalar_mul(pbf[:, :], sim[:, :], nsc[:, 3:4]))
                    for jt in range(njt):
                        p.t(lambda e, jt=jt: e.transpose(pTb[:, :], pbf[:, jt * 128:(jt + 1) * 128], eyeb[:, :]))
                        p.v(lambda e, jt=jt: e.tensor_copy(ptall[:, jt * 128:(jt + 1) * 128], pTb[:, :]))
                    for jt in range(njt):
                        p.t(lambda e, kvh=kvh, jt=jt, njt=njt: e.matmul(
                            pO[:, :], ptall[:, jt * 128:(jt + 1) * 128], vj[kvh][:, jt * 64:(jt + 1) * 64],
                            start=(jt == 0), stop=(jt == njt - 1)))
                    if g == 0:
                        p.v(lambda e: e.tensor_copy(ot[:, :], pO[:, :]))
                    else:
                        p.v(lambda e: e.tensor_add(ot[:, :], ot[:, :], pO[:, :]))
                p.t(lambda e: e.transpose(pTr[0:64, :], ot[:, 0:64], eyef[:, :]))
                p.v(lambda e, kvh=kvh: e.tensor_copy(atT[kvh][:, :], pTr[0:64, :]))
            for dh in range(2):
                for kvh in range(8):
                    p.t(lambda e, dh=dh, kvh=kvh: e.matmul(
                        pPr[:, :], atT[kvh][:, :], wob[kvh][:, dh * 512:(dh + 1) * 512],
                        start=(kvh == 0), stop=(kvh == 7)))
                p.v(lambda e, dh=dh: e.tensor_copy(stg[:, dh * 512:(dh + 1) * 512], pPr[:, :]))
            p.st(lambda e, base=base, it=it: e.dma_start(
                out=s1o[base + it * 128: base + (it + 1) * 128, :], in_=stg[:, :]))
    p.finish()
    es.close()
    return nc


def build_bitffn():
    nc = bass.Bass("TRN2", target_bir_lowering=False)
    xnT = nc.dram_tensor("xnT", [D, TPC], F32, kind="ExternalInput").ap()
    x1 = nc.dram_tensor("x1", [TPC, D], F32, kind="ExternalInput").ap()
    s1 = nc.dram_tensor("s1", [TPC, D], F32, kind="ExternalInput").ap()
    WfT = nc.dram_tensor("WfT", [D, 4096], F32, kind="ExternalInput").ap()
    WpT = nc.dram_tensor("WpT", [4096, D], F32, kind="ExternalInput").ap()
    LNW = nc.dram_tensor("LNW", [128, 32], F32, kind="ExternalInput").ap()
    LNB = nc.dram_tensor("LNB", [128, 32], F32, kind="ExternalInput").ap()
    ONE = nc.dram_tensor("ONE", [128, 128], F32, kind="ExternalInput").ap()
    x2o = nc.dram_tensor("x2o", [TPC, D], F32, kind="ExternalOutput").ap()

    es = ExitStack()
    sbt = lambda n, sh, dt=F32: es.enter_context(nc.sbuf_tensor(n, sh, dt))
    pst = lambda n, sh, dt=F32: es.enter_context(nc.psum_tensor(n, sh, dt))
    TC = 256
    wfq = [sbt(f"wfq{i}", [128, 4096], BF16) for i in range(8)]
    wpq = [sbt(f"wpq{i}", [128, 1024], BF16) for i in range(32)]
    stg = sbt("stg", [128, 2048], F32)
    qa = sbt("qa", [128, 2048], F32)
    xbq = [sbt(f"xbq{i}", [128, TC], BF16) for i in range(8)]
    hg = [sbt(f"hg{i}", [128, TC], BF16) for i in range(32)]
    sc1 = sbt("sc1", [128, TC], F32)
    sc2 = sbt("sc2", [128, TC], F32)
    mub = sbt("mub", [128, TC], F32)
    rsb = sbt("rsb", [128, TC], F32)
    row = sbt("row", [1, 4 * TC], F32)
    cols = sbt("cols", [128, 80], F32)
    ones = sbt("ones", [128, 128], F32)
    cmb0 = sbt("cmb0", [128, 1024], F32)
    cmb1 = sbt("cmb1", [128, 1024], F32)
    cmb2 = sbt("cmb2", [128, 1024], F32)
    pH = pst("pH", [128, TC])
    pSt = pst("pSt", [128, TC])
    pBc = pst("pBc", [128, TC])
    pP = [pst(f"pP{i}", [128, 512]) for i in range(2)]
    pG = pst("pG", [128, 8])
    p = Pr(nc, es)

    p.ld(lambda e: e.dma_start(out=ones[:, :], in_=ONE[:, :]))
    p.ld(lambda e: e.dma_start(out=cols[:, 0:32], in_=LNW[:, :]))
    p.ld(lambda e: e.dma_start(out=cols[:, 32:64], in_=LNB[:, :]))
    p.v(lambda e: e.memset(cols[:, 70:71], 1e-6))
    p.v(lambda e: e.memset(cols[:, 71:72], 1e-5))

    for (src, nt, wid, qdst) in [(WfT, 8, 4096, None), (WpT, 32, 1024, None)]:
        pass
    for w_i in range(2):
        src = WfT if w_i == 0 else WpT
        nt = 8 if w_i == 0 else 32
        wid = 4096 if w_i == 0 else 1024
        qdst = None
        nh = wid // 2048 if wid >= 2048 else 1
        hw = min(wid, 2048)
        p.v(lambda e: e.memset(cols[:, 64:65], 0.0))
        for t in range(nt):
            for h in range(nh):
                p.ld(lambda e, src=src, t=t, h=h, hw=hw: e.dma_start(
                    out=stg[:, 0:hw], in_=src[t * 128:(t + 1) * 128, h * hw:(h + 1) * hw]))
                p.v(lambda e, hw=hw: e.reduce_sum(cols[:, 65:66], stg[:, 0:hw], AX,
                                                  apply_absolute_value=True))
                p.v(lambda e: e.tensor_add(cols[:, 64:65], cols[:, 64:65], cols[:, 65:66]))
        p.t(lambda e: e.matmul(pG[0:1, 0:1], ones[:, 0:1], cols[:, 64:65], start=True, stop=True))
        p.s(lambda e, nt=nt, wid=wid: e.activation(cols[0:1, 66:67], pG[0:1, 0:1], AF.Identity,
                                                   bias=cols[0:1, 70:71], scale=1.0 / (128.0 * nt * wid)))
        p.v(lambda e: e.tensor_scalar_mul(cols[0:1, 67:68], cols[0:1, 66:67], 0.5))
        p.t(lambda e: e.matmul(pG[:, 1:2], ones[0:1, :], cols[0:1, 67:68], start=True, stop=True))
        p.v(lambda e: e.tensor_copy(cols[:, 68:69], pG[:, 1:2]))
        p.v(lambda e: e.tensor_scalar_mul(cols[:, 69:70], cols[:, 68:69], -1.0))
        thr = cols[:, 68:69]
        nthr = cols[:, 69:70]
        for t in range(nt):
            for h in range(nh):
                p.ld(lambda e, src=src, t=t, h=h, hw=hw: e.dma_start(
                    out=stg[:, 0:hw], in_=src[t * 128:(t + 1) * 128, h * hw:(h + 1) * hw]))
                p.v(lambda e, hw=hw, thr=thr: e.tensor_scalar(qa[:, 0:hw], stg[:, 0:hw], thr, None, OP.is_gt))
                p.v(lambda e, hw=hw, nthr=nthr: e.tensor_scalar(stg[:, 0:hw], stg[:, 0:hw], nthr, None, OP.is_lt))
                if w_i == 0:
                    p.v(lambda e, t=t, h=h, hw=hw: e.tensor_tensor(
                        wfq[t][:, h * hw:(h + 1) * hw], qa[:, 0:hw], stg[:, 0:hw], OP.subtract))
                else:
                    p.v(lambda e, t=t, hw=hw: e.tensor_tensor(
                        wpq[t][:, 0:hw], qa[:, 0:hw], stg[:, 0:hw], OP.subtract))

    for tc in range(TPC // TC):
        t0 = tc * TC
        for dt_i in range(8):
            p.ld(lambda e, dt_i=dt_i, t0=t0: e.dma_start(
                out=stg[:, 0:TC], in_=xnT[dt_i * 128:(dt_i + 1) * 128, t0:t0 + TC]))
            p.s(lambda e, dt_i=dt_i: e.activation(xbq[dt_i][:, :], stg[:, 0:TC], AF.Identity))
        p.v(lambda e: e.memset(row[0:1, 0:2 * TC], 0.0))
        for ht in range(32):
            for dt_i in range(8):
                p.t(lambda e, ht=ht, dt_i=dt_i: e.matmul(
                    pH[:, :], wfq[dt_i][:, ht * 128:(ht + 1) * 128], xbq[dt_i][:, :],
                    start=(dt_i == 0), stop=(dt_i == 7)))
            p.s(lambda e: e.activation(sc1[:, :], pH[:, :], AF.Gelu))
            p.s(lambda e: e.activation(sc2[:, :], sc1[:, :], AF.Square))
            p.s(lambda e, ht=ht: e.activation(hg[ht][:, :], sc1[:, :], AF.Identity))
            p.t(lambda e: e.matmul(pSt[0:1, :], ones[:, 0:1], sc1[:, :], start=True, stop=True))
            p.v(lambda e: e.tensor_tensor(row[0:1, 0:TC], row[0:1, 0:TC], pSt[0:1, :], OP.add))
            p.t(lambda e: e.matmul(pSt[0:1, :], ones[:, 0:1], sc2[:, :], start=True, stop=True))
            p.v(lambda e: e.tensor_tensor(row[0:1, TC:2 * TC], row[0:1, TC:2 * TC], pSt[0:1, :], OP.add))
        p.v(lambda e: e.tensor_scalar_mul(row[0:1, 0:TC], row[0:1, 0:TC], 1.0 / 4096.0))
        p.v(lambda e: e.tensor_scalar_mul(row[0:1, TC:2 * TC], row[0:1, TC:2 * TC], 1.0 / 4096.0))
        p.v(lambda e: e.tensor_mul(row[0:1, 2 * TC:3 * TC], row[0:1, 0:TC], row[0:1, 0:TC]))
        p.v(lambda e: e.tensor_sub(row[0:1, TC:2 * TC], row[0:1, TC:2 * TC], row[0:1, 2 * TC:3 * TC]))
        p.s(lambda e: e.activation(row[0:1, 2 * TC:3 * TC], row[0:1, TC:2 * TC], AF.Sqrt, bias=cols[0:1, 71:72]))
        p.v(lambda e: e.reciprocal(row[0:1, 3 * TC:4 * TC], row[0:1, 2 * TC:3 * TC]))
        p.t(lambda e: e.matmul(pBc[:, :], ones[0:1, :], row[0:1, 0:TC], start=True, stop=True))
        p.v(lambda e: e.tensor_copy(mub[:, :], pBc[:, :]))
        p.t(lambda e: e.matmul(pBc[:, :], ones[0:1, :], row[0:1, 3 * TC:4 * TC], start=True, stop=True))
        p.v(lambda e: e.tensor_copy(rsb[:, :], pBc[:, :]))
        for ht in range(32):
            p.s(lambda e, ht=ht: e.activation(sc1[:, :], hg[ht][:, :], AF.Identity))
            p.v(lambda e: e.tensor_sub(sc1[:, :], sc1[:, :], mub[:, :]))
            p.v(lambda e: e.tensor_mul(sc1[:, :], sc1[:, :], rsb[:, :]))
            p.v(lambda e, ht=ht: e.tensor_scalar(hg[ht][:, :], sc1[:, :], cols[:, ht:ht + 1],
                                                 cols[:, 32 + ht:33 + ht], OP.mult, OP.add))
        for ts2 in range(2):
            for dh in range(2):
                for ht in range(32):
                    p.t(lambda e, ts2=ts2, dh=dh, ht=ht: e.matmul(
                        pP[dh][:, :], hg[ht][:, ts2 * 128:(ts2 + 1) * 128],
                        wpq[ht][:, dh * 512:(dh + 1) * 512], start=(ht == 0), stop=(ht == 31)))
            tt = t0 + ts2 * 128
            p.ld(lambda e, tt=tt: e.dma_start(out=cmb0[:, :], in_=x1[tt:tt + 128, :]))
            p.ld(lambda e, tt=tt: e.dma_start(out=cmb1[:, :], in_=s1[tt:tt + 128, :]))
            p.s(lambda e: e.activation(cmb2[:, 0:512], pP[0][:, :], AF.Silu))
            p.s(lambda e: e.activation(cmb2[:, 512:1024], pP[1][:, :], AF.Silu))
            p.v(lambda e: e.tensor_mul(cmb2[:, :], cmb1[:, :], cmb2[:, :]))
            p.v(lambda e: e.tensor_add(cmb2[:, :], cmb0[:, :], cmb2[:, :]))
            p.st(lambda e, tt=tt: e.dma_start(out=x2o[tt:tt + 128, :], in_=cmb2[:, :]))
    p.finish()
    es.close()
    return nc


def _run(nc, in_maps, outs):
    res = run_bass_kernel_spmd(nc, in_maps, core_ids=list(range(NC)))
    return [[np.asarray(r[o]) for o in outs] for r in res.results]


def kernel(**inp):
    f32 = np.float32
    x0 = np.ascontiguousarray(inp["L0_attn_x"].astype(f32, copy=False))
    x2d = x0.reshape(B * L, D)

    nc = build_ln0()
    maps = [{"x": np.ascontiguousarray(x2d[i * TPC:(i + 1) * TPC])} for i in range(NC)]
    outs = _run(nc, maps, ["y"])
    xn0 = np.concatenate([o[0] for o in outs], axis=0)

    l_ar = np.arange(L, dtype=np.float64)
    f_ar = np.arange(NF, dtype=np.float64)
    ang = 2.0 * np.pi * np.outer(l_ar, f_ar) / (2 * L)
    fmask = (f_ar <= L).astype(np.float64)
    DFC = (np.cos(ang) * fmask).astype(f32)
    DFS = (-np.sin(ang) * fmask).astype(f32)
    wf = np.where((f_ar == 0) | (f_ar == L), 1.0, 2.0) * fmask / (2 * L)
    ICR = (np.cos(ang.T) * wf[:, None]).astype(f32)
    ICI = (-np.sin(ang.T) * wf[:, None]).astype(f32)
    iota = np.broadcast_to(np.arange(L, dtype=f32), (128, L)).copy()
    eyef = np.eye(128, dtype=f32)
    xn3 = xn0.reshape(B, L, D)
    nc = build_ssm()
    maps = []
    for i in range(NC):
        hs = slice(i * 128, (i + 1) * 128)
        xh = np.ascontiguousarray(xn3[:, :, hs].transpose(1, 0, 2).reshape(L, 2048))
        maps.append({
            "xh": xh, "dfc": DFC, "dfs": DFS, "icr": ICR, "ici": ICI,
            "logdt": np.ascontiguousarray(inp["L0_attn_log_dt"][hs].reshape(128, 1)),
            "logA": np.ascontiguousarray(inp["L0_attn_log_A_real"][hs]),
            "AIm": np.ascontiguousarray(inp["L0_attn_A_imag"][hs]),
            "CRe": np.ascontiguousarray(inp["L0_attn_C"][hs, :, 0]),
            "CIm": np.ascontiguousarray(inp["L0_attn_C"][hs, :, 1]),
            "Dc": np.ascontiguousarray(inp["L0_attn_D"][hs].reshape(128, 1)),
            "iot": iota, "eye": eyef,
        })
    outs = _run(nc, maps, ["st"])
    s0 = np.empty((B, L, D), dtype=f32)
    for i in range(NC):
        s0[:, :, i * 128:(i + 1) * 128] = outs[i][0].reshape(L, B, 128).transpose(1, 0, 2)
    s0_2d = s0.reshape(B * L, D)

    W1 = np.ascontiguousarray(np.concatenate(
        [inp["L0_ffn_w1"], inp["L0_ffn_sw1"][None]], axis=0).reshape(9 * 1024, 1408))
    W3 = np.ascontiguousarray(np.concatenate(
        [inp["L0_ffn_w3"], inp["L0_ffn_sw3"][None]], axis=0).reshape(9 * 1024, 1408))
    W2 = np.ascontiguousarray(np.concatenate(
        [inp["L0_ffn_w2"], inp["L0_ffn_sw2"][None]], axis=0).reshape(9 * 1408, 1024))
    B1 = np.ascontiguousarray(np.concatenate(
        [inp["L0_ffn_b1"], inp["L0_ffn_sb1"][None]], axis=0).reshape(9, 11, 128)
        .transpose(0, 2, 1).reshape(9 * 128, 11))
    B3 = np.ascontiguousarray(np.concatenate(
        [inp["L0_ffn_b3"], inp["L0_ffn_sb3"][None]], axis=0).reshape(9, 11, 128)
        .transpose(0, 2, 1).reshape(9 * 128, 11))
    B2 = np.ascontiguousarray(np.concatenate([inp["L0_ffn_b2"], inp["L0_ffn_sb2"][None]], axis=0))
    GW = np.ascontiguousarray(inp["L0_ffn_gate_w"].T)
    GB = np.ascontiguousarray(inp["L0_ffn_gate_b"].reshape(1, 8))
    ONE = np.ones((128, 128), dtype=f32)
    nc = build_moe()
    maps = []
    for i in range(NC):
        ts = slice(i * TPC, (i + 1) * TPC)
        maps.append({
            "xnT": np.ascontiguousarray(xn0[ts].T), "x0": np.ascontiguousarray(x2d[ts]),
            "s0": np.ascontiguousarray(s0_2d[ts]),
            "W1": W1, "W3": W3, "W2": W2, "B1": B1, "B3": B3, "B2": B2,
            "GW": GW, "GB": GB, "ONE": ONE,
        })
    outs = _run(nc, maps, ["x1o", "xn1o"])
    x1 = np.concatenate([o[0] for o in outs], axis=0)
    xn1 = np.concatenate([o[1] for o in outs], axis=0)

    itile = np.arange(128)
    jcol = np.arange(4096) - 2048
    MBc = np.where(jcol[None, :] - itile[:, None] > 0, np.float32(-1e30), np.float32(0.0)).astype(f32)
    EYE2c = np.zeros((128, 64), dtype=f32)
    for pp in range(128):
        EYE2c[pp, pp % 64] = 1.0
    EYE2c = EYE2c.astype(BFNP)
    nc = build_attn()
    maps = []
    for i in range(NC):
        ts = slice(i * TPC, (i + 1) * TPC)
        maps.append({
            "xnT": np.ascontiguousarray(xn1[ts].T),
            "Wq": np.ascontiguousarray(inp["L1_attn_Wq"]),
            "Wkv": np.ascontiguousarray(inp["L1_attn_Wkv"]),
            "Wo": np.ascontiguousarray(inp["L1_attn_Wout"]),
            "NW": np.ascontiguousarray(inp["L1_attn_norm_w"].reshape(8, 128).T),
            "QG": np.ascontiguousarray(inp["L1_attn_q_gamma"].reshape(1024).reshape(8, 128).T),
            "KG": np.ascontiguousarray(inp["L1_attn_k_gamma"].reshape(512).reshape(4, 128).T),
            "ONE": ONE, "EYEF": eyef, "EYEB": eyef.astype(BFNP),
            "EYE2": EYE2c, "MB": MBc,
        })
    outs = _run(nc, maps, ["s1o"])
    s1 = np.concatenate([o[0] for o in outs], axis=0)

    WfT = np.ascontiguousarray(inp["L1_ffn_W_fc"].T)
    WpT = np.ascontiguousarray(inp["L1_ffn_W_proj"].T)
    LNW = np.ascontiguousarray(inp["L1_ffn_ln_w"].reshape(32, 128).T)
    LNB = np.ascontiguousarray(inp["L1_ffn_ln_b"].reshape(32, 128).T)
    nc = build_bitffn()
    maps = []
    for i in range(NC):
        ts = slice(i * TPC, (i + 1) * TPC)
        maps.append({
            "xnT": np.ascontiguousarray(xn1[ts].T),
            "x1": np.ascontiguousarray(x1[ts]), "s1": np.ascontiguousarray(s1[ts]),
            "WfT": WfT, "WpT": WpT, "LNW": LNW, "LNB": LNB, "ONE": ONE,
        })
    outs = _run(nc, maps, ["x2o"])
    x2 = np.concatenate([o[0] for o in outs], axis=0)
    return x2.reshape(B, L, D)
